# version 08 (final)
# baseline (speedup 1.0000x reference)
import contextlib
import numpy as np
import ml_dtypes
import concourse.bass as bass
import concourse.mybir as mybir
from concourse.bass_utils import run_bass_kernel_spmd

F32 = mybir.dt.float32
BF16 = mybir.dt.bfloat16
AF = mybir.ActivationFunctionType
ALU = mybir.AluOpType
AX = mybir.AxisListType
BF16NP = ml_dtypes.bfloat16

DEBUG = False
NCORE = 8
D = 1024
DC = 8
TPC = 4096
CH = 512
NCH = TPC // CH
SEQ = 2048
NST = 16
RK = 64
EPS = 1e-5

ENGINES = ("sync", "tensor", "scalar", "vector", "gpsimd")


class PB:

    def __init__(self, nc):
        self.nc = nc
        self.ops = {e: [] for e in ENGINES}
        self.cnt = {}
        self.sems = {}
        self._stack = contextlib.ExitStack()

    def sem(self, name):
        if name not in self.sems:
            self.sems[name] = self._stack.enter_context(self.nc.semaphore(name))
            self.cnt[name] = 0
        return self.sems[name]

    def rec(self, engine, fn, waits=(), inc=None, n=1):
        waits = [(s, v) for (s, v) in waits if v is not None]
        for s, v in waits:
            self.sem(s)
            assert v <= self.cnt[s], f"wait on future {s}:{v} (have {self.cnt[s]})"
        after = None
        if inc is not None:
            self.sem(inc)
            self.cnt[inc] += n
            after = self.cnt[inc]
        self.ops[engine].append((tuple(waits), fn, inc, n))
        return after

    def dma(self, queue, out, in_, sem, waits=()):
        fn = lambda e, out=out, in_=in_: e.dma_start(out=out, in_=in_)
        return self.rec(queue, fn, waits=waits, inc=sem, n=16)

    def barrier(self):
        self.sem("BAR")
        target = self.cnt["BAR"] + len(ENGINES)
        for eng in ENGINES:
            self.rec(eng, lambda e: e.nop(), inc="BAR")
        for eng in ENGINES:
            self.rec(eng, None, waits=[("BAR", target)])

    def emit(self):
        block = self._stack.enter_context(self.nc.Block())
        for eng in ENGINES:
            ops = self.ops[eng]
            if not ops:
                continue

            def emitter(e, ops=ops):
                for waits, fn, inc, n in ops:
                    for s, v in waits:
                        e.wait_ge(self.sems[s], v)
                    inst = fn(e) if fn is not None else None
                    if inc is not None:
                        assert inst is not None
                        inst.then_inc(self.sems[inc], n)
            getattr(block, eng)(emitter)

    def close(self):
        self._stack.close()


def _bf(x):
    return np.ascontiguousarray(x.astype(BF16NP))


def make_consts():
    En = np.zeros((16, 16 * 128), np.float32)
    for n in range(16):
        En[n, n * 128:(n + 1) * 128] = 1.0
    E8 = np.zeros((8, 8 * 128), np.float32)
    for n in range(8):
        E8[n, n * 128:(n + 1) * 128] = 1.0
    return {
        "cEn": _bf(En),
        "cE8": _bf(E8),
        "cOnesRowF": np.ones((1, 128), np.float32),
        "cOnesColF": np.ones((128, 1), np.float32),
        "cOnesColB": _bf(np.ones((128, 1), np.float32)),
        "cIdentF": np.eye(128, dtype=np.float32),
        "cBias": np.tile(np.array([[EPS, 1.0]], np.float32), (128, 1)),
    }



def emit_phase_A(pb, nc, st, io):
    E = st.enter_context
    x_fm = io["x_fm"]
    x1, ln1b = io["x1"], io["ln1b"]
    QCH = CH // 4

    En_s = E(nc.sbuf_tensor("aEn", [16, 16 * 128], BF16))
    o1f = E(nc.sbuf_tensor("aO1f", [1, 128], F32))
    oKf = E(nc.sbuf_tensor("aOKf", [128, 1], F32))
    oKb = E(nc.sbuf_tensor("aOKb", [128, 1], BF16))
    Wx_s = E(nc.sbuf_tensor("aWxS", [128, DC, 96], BF16))
    Wdt_s = E(nc.sbuf_tensor("aWdtS", [64, DC, 128], BF16))
    Bdt_s = E(nc.sbuf_tensor("aBdtS", [128, DC], F32))
    Dcol_s = E(nc.sbuf_tensor("aDS", [128, DC], F32))
    A_s = E(nc.sbuf_tensor("aAS", [128, DC, NST], F32))
    xa = E(nc.sbuf_tensor("aXA", [128, DC, CH], F32))
    xr = [E(nc.sbuf_tensor(f"aXR{r}", [128, CH], F32)) for r in range(2)]
    sqr = [E(nc.sbuf_tensor(f"aSQR{r}", [128, CH], BF16)) for r in range(2)]
    ln0w = [E(nc.sbuf_tensor(f"aLN0{p}", [128, DC, CH], BF16)) for p in range(2)]
    dtw = E(nc.sbuf_tensor("aDT", [128, DC, CH], F32))
    dtrw = [E(nc.sbuf_tensor(f"aDTR{r}", [64, CH], BF16)) for r in range(2)]
    Bmw = [E(nc.sbuf_tensor(f"aBM{r}", [16, CH], BF16)) for r in range(2)]
    Cmw = [E(nc.sbuf_tensor(f"aCM{r}", [16, CH], BF16)) for r in range(2)]
    Bb = E(nc.sbuf_tensor("aBB", [128, NST, CH], BF16))
    Ca = E(nc.sbuf_tensor("aCA", [128, CH, NST], BF16))
    h_all = [E(nc.sbuf_tensor(f"aH{q}", [128, CH, NST], BF16)) for q in range(2)]
    hc = [E(nc.sbuf_tensor(f"aHC{q}", [128, QCH, NST], BF16)) for q in range(2)]
    state_sb = E(nc.sbuf_tensor("aST", [128, DC, NST], F32))
    ww = [E(nc.sbuf_tensor(f"aWW{q}", [128, CH], BF16)) for q in range(2)]
    dAt = [E(nc.sbuf_tensor(f"aDA{r}", [128, CH], F32)) for r in range(4)]
    dBt = [E(nc.sbuf_tensor(f"aDB{r}", [128, CH], BF16)) for r in range(4)]
    yv = [E(nc.sbuf_tensor(f"aYV{q}", [128, CH], F32)) for q in range(2)]
    x1w = E(nc.sbuf_tensor("aX1", [128, DC, CH], F32))
    ln1r = [E(nc.sbuf_tensor(f"aLN1{r}", [128, CH], BF16)) for r in range(2)]
    tpa = [E(nc.sbuf_tensor(f"aTPA{q}", [128, CH], BF16)) for q in range(2)]
    tpb = [E(nc.sbuf_tensor(f"aTPB{q}", [128, CH], BF16)) for q in range(2)]
    rowm = E(nc.sbuf_tensor("aRowM", [1, 2, CH], F32))
    rowq = E(nc.sbuf_tensor("aRowQ", [1, 2, CH], F32))
    rowr = E(nc.sbuf_tensor("aRowR", [1, 2, CH], F32))
    rowb = E(nc.sbuf_tensor("aRowB", [1, 2, CH], F32))
    ab_bc = E(nc.sbuf_tensor("aABC", [128, 2, CH], F32))
    cbias = E(nc.sbuf_tensor("aCB", [128, 2], F32))

    ps_sum = E(nc.psum_tensor("apSum", [1, CH], F32))
    ps_sq = E(nc.psum_tensor("apSq", [1, CH], F32))
    ps_a = E(nc.psum_tensor("apA", [128, CH], F32))
    ps_b = E(nc.psum_tensor("apB", [128, CH], F32))
    ps_dtr = E(nc.psum_tensor("apDtr", [64, CH], F32))
    ps_B = E(nc.psum_tensor("apBm", [16, CH], F32))
    ps_C = E(nc.psum_tensor("apCm", [16, CH], F32))
    ps_dt = E(nc.psum_tensor("apDt", [128, CH], F32))

    S = pb.rec
    if DEBUG:
        dbg = {k: io[k] for k in ("dbg_ln0", "dbg_dt", "dbg_Bb", "dbg_Ca", "dbg_h", "dbg_y",
                                  "dbg_dtr", "dbg_tpa", "dbg_ps")}

    pb.dma("sync", En_s[:], io["cEn"], "dC")
    pb.dma("sync", o1f[:], io["cOnesRowF"], "dC")
    pb.dma("sync", oKf[:], io["cOnesColF"], "dC")
    pb.dma("sync", oKb[:], io["cOnesColB"], "dC")
    pb.dma("sync", Bdt_s[:], io["aBdt"], "dC")
    pb.dma("sync", Dcol_s[:], io["aD"], "dC")
    pb.dma("sync", A_s[:], io["aAlog"].rearrange("(dc p) n -> p dc n", p=128), "dC")
    pb.dma("sync", cbias[:], io["cBias"], "dC")
    pb.dma("gpsimd", Wx_s[:], io["aWx"].rearrange("(dc p) m -> p dc m", p=128), "dG")
    pb.dma("gpsimd", Wdt_s[:], io["aWdt"].rearrange("r (dc p) -> r dc p", dc=DC), "dG")
    setup_dC = pb.cnt["dC"]
    setup_dG = pb.cnt["dG"]
    if "_prelude" in io:
        io["_prelude"](pb)
    S("scalar", lambda e: e.activation(A_s[:], A_s[:], AF.Exp),
      waits=[("dC", setup_dC)], inc="ACT")
    a_neg = S("vector", lambda e: e.tensor_scalar(A_s[:], A_s[:], -1.0, None, ALU.mult),
              waits=[("ACT", pb.cnt["ACT"])], inc="DVE")

    ln0_done_prev = None
    free_ln0 = [None, None]
    free_dt = None
    free_dt_w = None
    cons_dA = [None] * 4
    cons_dB = [None] * 4
    cons_h = [None] * 2
    cons_hc = [None, None]
    cons_sq = [None, None]
    cons_dtr = [None, None]
    cons_bm = [None, None]
    cons_xr = [None, None]
    cons_ln1r = [None, None]
    scan_last = None
    hmul_last = None
    row_done = None
    ln1_done = None
    xp_evac = None
    bank_ab_free = None
    dOx_before = 0

    for c in range(NCH):
        p = c % 2
        tslc = slice(c * CH, (c + 1) * CH)
        batch_start = (c % 4 == 0)

        ldx = pb.dma("sync", xa[:],
                     x_fm.rearrange("(dc p) t -> p dc t", p=128)[:, :, tslc],
                     "dXA", waits=[("dC", setup_dC), ("DVE", ln0_done_prev)])

        stats_pe = None
        for dc in range(DC):
            r = dc % 2
            sq_done = S("scalar", lambda e, dc=dc, r=r: e.activation(
                sqr[r][:], xa[:, dc, :], AF.Square),
                waits=[("dXA", ldx), ("PE", cons_sq[r])], inc="ACT")
            wts = [("ACT", sq_done)] + ([("DVE", row_done)] if dc == 0 else [])
            S("tensor", lambda e, dc=dc: e.matmul(
                ps_sum[0:1, :], oKf[:], xa[:, dc, :], start=(dc == 0), stop=(dc == DC - 1)),
              waits=wts)
            stats_pe = S("tensor", lambda e, r=r, dc=dc: e.matmul(
                ps_sq[0:1, :], oKb[:], sqr[r][:], start=(dc == 0), stop=(dc == DC - 1)),
                inc="PE")
            cons_sq[r] = stats_pe

        S("vector", lambda e, p=p: e.tensor_scalar(rowm[:, p, :], ps_sum[0:1, :], 1.0 / D, None, ALU.mult),
          waits=[("PE", stats_pe)])
        S("vector", lambda e, p=p: e.tensor_scalar(rowq[:, p, :], ps_sq[0:1, :], 1.0 / D, None, ALU.mult))
        S("vector", lambda e, p=p: e.tensor_tensor(rowb[:, p, :], rowm[:, p, :], rowm[:, p, :], ALU.mult))
        var_done = S("vector", lambda e, p=p: e.tensor_tensor(rowq[:, p, :], rowq[:, p, :], rowb[:, p, :], ALU.subtract),
                     inc="DVE")
        S("scalar", lambda e, p=p: e.activation(rowq[:, p, :], rowq[:, p, :], AF.Ln, bias=cbias[0:1, 0:1]),
          waits=[("DVE", var_done)])
        rstd_act = S("scalar", lambda e, p=p: e.activation(rowr[:, p, :], rowq[:, p, :], AF.Exp, scale=-0.5),
                     inc="ACT")
        row_done = S("vector", lambda e, p=p: e.scalar_tensor_tensor(
            rowb[:, p, :], rowm[:, p, :], -1.0, rowr[:, p, :], ALU.mult, ALU.mult),
            waits=[("ACT", rstd_act)], inc="DVE")

        S("tensor", lambda e, p=p: e.matmul(ps_a[:], o1f[:], rowr[:, p, :], start=True, stop=True),
          waits=[("DVE", row_done), ("ACT", bank_ab_free)])
        bc_pe = S("tensor", lambda e, p=p: e.matmul(ps_b[:], o1f[:], rowb[:, p, :], start=True, stop=True),
                  inc="PE")
        S("scalar", lambda e: e.copy(ab_bc[:, 0, :], ps_a[:]),
          waits=[("PE", bc_pe), ("DVE", ln1_done)])
        ab_act = S("scalar", lambda e: e.copy(ab_bc[:, 1, :], ps_b[:]), inc="ACT")
        bank_ab_free = ab_act

        ln0_done = None
        for dc in range(DC):
            wts = [("ACT", ab_act), ("POOL", free_ln0[p])] if dc == 0 else []
            S("vector", lambda e, p=p, dc=dc: e.tensor_tensor(
                ln0w[p][:, dc, :], xa[:, dc, :], ab_bc[:, 0, :], ALU.mult), waits=wts)
            ln0_done = S("vector", lambda e, p=p, dc=dc: e.tensor_tensor(
                ln0w[p][:, dc, :], ln0w[p][:, dc, :], ab_bc[:, 1, :], ALU.add),
                inc="DVE" if dc == DC - 1 else None)
        ln0_done_prev = ln0_done
        if DEBUG and c == 0:
            pb.dma("sync", dbg["dbg_ln0"], ln0w[p][:], "dDB", waits=[("DVE", ln0_done)])

        rr = c % 2
        xp_pe = None
        for k in range(DC):
            wts = [("DVE", ln0_done), ("ACT", xp_evac), ("dG", setup_dG)] if k == 0 else []
            S("tensor", lambda e, p=p, k=k: e.matmul(
                ps_dtr[:], Wx_s[:, k, 0:64], ln0w[p][:, k, :], start=(k == 0), stop=(k == DC - 1)),
              waits=wts)
            S("tensor", lambda e, p=p, k=k: e.matmul(
                ps_B[0:16, :], Wx_s[:, k, 64:80], ln0w[p][:, k, :], start=(k == 0), stop=(k == DC - 1)))
            xp_pe = S("tensor", lambda e, p=p, k=k: e.matmul(
                ps_C[0:16, :], Wx_s[:, k, 80:96], ln0w[p][:, k, :], start=(k == 0), stop=(k == DC - 1)),
                inc="PE" if k == DC - 1 else None)
        S("scalar", lambda e, rr=rr: e.copy(dtrw[rr][:], ps_dtr[:]),
          waits=[("PE", xp_pe), ("PE", cons_dtr[rr]), ("PE", cons_bm[rr])])
        S("scalar", lambda e, rr=rr: e.copy(Bmw[rr][:], ps_B[0:16, :]))
        xp_evac = S("scalar", lambda e, rr=rr: e.copy(Cmw[rr][:], ps_C[0:16, :]), inc="ACT")

        dt_done = None
        dtp_evac = None
        sp_cons = [None, None]
        sp_consa = [None, None]
        for dc in range(DC):
            q = dc % 2
            dt_pe = S("tensor", lambda e, rr=rr, dc=dc: e.matmul(
                ps_dt[:], Wdt_s[:, dc, :], dtrw[rr][:], start=True, stop=True),
                waits=[("ACT", xp_evac), ("DVE", dtp_evac)], inc="PE")
            cons_dtr[rr] = dt_pe
            if DEBUG and c == 0 and dc == 0:
                dps = S("scalar", lambda e: e.copy(tpb[1][:], ps_dt[:]),
                        waits=[("PE", dt_pe)], inc="ACT")
                pb.dma("sync", dbg["dbg_ps"], tpb[1][:], "dDB", waits=[("ACT", dps)])
                pb.dma("sync", dbg["dbg_dtr"], dtrw[rr][:], "dDB", waits=[("ACT", xp_evac)])
            dtp_evac = S("vector", lambda e, dc=dc, q=q: e.tensor_scalar(
                tpa[q][:], ps_dt[:], Bdt_s[:, dc:dc + 1], None, ALU.add),
                waits=[("PE", dt_pe), ("DVE", sp_cons[q]), ("ACT", sp_consa[q])]
                      + ([("dDB", pb.cnt["dDB"])] if DEBUG and c == 0 and dc == 0 else []),
                inc="DVE")
            if DEBUG and c == 0 and dc == 0:
                pb.dma("sync", dbg["dbg_tpa"], tpa[q][:], "dDB", waits=[("DVE", dtp_evac)])
            abs_done = S("scalar", lambda e, q=q: e.activation(tpb[q][:], tpa[q][:], AF.Abs),
                         waits=[("DVE", dtp_evac)], inc="ACT")
            S("scalar", lambda e, q=q: e.activation(tpb[q][:], tpb[q][:], AF.Exp, scale=-1.0))
            l_ = S("scalar", lambda e, q=q: e.activation(tpb[q][:], tpb[q][:], AF.Ln, bias=cbias[:, 1:2]),
                   inc="ACT")
            sp_consa[q] = l_
            rl_ = S("scalar", lambda e, q=q: e.activation(
                tpa[q][:], tpa[q][:], AF.Relu), waits=[("DVE", dtp_evac)], inc="ACT")
            wts = [("ACT", rl_)] + ([("ACT", free_dt), ("POOL", free_dt_w)] if dc == 0 else [])
            dt_done = S("vector", lambda e, p=p, dc=dc, q=q: e.tensor_tensor(
                dtw[:, dc, :], tpa[q][:], tpb[q][:], ALU.add), waits=wts, inc="DVE")
            sp_cons[q] = dt_done

        if DEBUG and c == 0:
            pb.dma("sync", dbg["dbg_dt"], dtw[:], "dDB", waits=[("DVE", dt_done)])
        bc_evac = None
        for n in range(NST):
            wts = ([("ACT", bank_ab_free), ("DVE", scan_last), ("POOL", hmul_last)]
                   if n == 0 else [("ACT", bc_evac)])
            S("tensor", lambda e, rr=rr, n=n: e.matmul(
                ps_a[:], En_s[:, n * 128:(n + 1) * 128], Bmw[rr][:], start=True, stop=True),
              waits=wts)
            peC = S("tensor", lambda e, rr=rr, n=n: e.matmul(
                ps_b[:], En_s[:, n * 128:(n + 1) * 128], Cmw[rr][:], start=True, stop=True),
                inc="PE")
            if n == NST - 1:
                cons_bm[rr] = peC
            S("scalar", lambda e, n=n: e.copy(Bb[:, n, :], ps_a[:]),
              waits=[("PE", peC)])
            bc_evac = S("scalar", lambda e, n=n: e.copy(Ca[:, :, n], ps_b[:]),
                        inc="ACT")
        bank_ab_free = bc_evac
        if DEBUG and c == 0:
            pb.dma("sync", dbg["dbg_Bb"], Bb[:], "dDB", waits=[("ACT", bc_evac)])
            pb.dma("sync", dbg["dbg_Ca"], Ca[:], "dDB")

        pending_tail = None
        for dc in range(DC):
            q = dc % 2
            w_done = S("gpsimd", lambda e, p=p, dc=dc, q=q: e.tensor_tensor(
                ww[q][:], dtw[:, dc, :], ln0w[p][:, dc, :], ALU.mult),
                waits=[("DVE", dt_done), ("DVE", ln0_done)], inc="POOL")
            sc = None
            for n in range(NST):
                r = n % 4
                wts = [("DVE", cons_dA[r])]
                if n == 0 and dc == 0 and c == 0:
                    wts.append(("DVE", a_neg))
                dA_done = S("scalar", lambda e, p=p, dc=dc, n=n, r=r: e.activation(
                    dAt[r][:], dtw[:, dc, :], AF.Exp, scale=A_s[:, dc, n:n + 1]),
                    waits=wts, inc="ACT")
                dB_done = S("gpsimd", lambda e, p=p, n=n, r=r, q=q: e.tensor_tensor(
                    dBt[r][:], ww[q][:], Bb[:, n, :], ALU.mult),
                    waits=[("POOL", w_done), ("ACT", bc_evac), ("DVE", cons_dB[r])],
                    inc="POOL")
                init = 0.0 if batch_start else state_sb[:, dc, n:n + 1]
                sc = S("vector", lambda e, q=q, n=n, r=r, init=init: e.tensor_tensor_scan(
                    h_all[q][:, :, n], dAt[r][:], dBt[r][:], init, ALU.mult, ALU.add),
                    waits=[("ACT", dA_done), ("POOL", dB_done), ("POOL", cons_h[q])],
                    inc="DVE")
                cons_dA[r] = sc
                cons_dB[r] = sc
            S("vector", lambda e, q=q, dc=dc: e.tensor_copy(
                state_sb[:, dc, :], h_all[q][:, CH - 1, :]), waits=[("DVE", sc)])
            if DEBUG and c == 0 and dc == 0:
                pb.dma("sync", dbg["dbg_h"], h_all[q][:], "dDB", waits=[("DVE", sc)])

            def tail(dc=dc, q=q, sc=sc, p=p, tslc=tslc):
                hmul = None
                for hh in range(4):
                    hs = slice(hh * QCH, (hh + 1) * QCH)
                    hb = hh % 2
                    hmul = S("gpsimd", lambda e, q=q, hb=hb, hs=hs: e.tensor_tensor(
                        hc[hb][:], h_all[q][:, hs, :], Ca[:, hs, :], ALU.mult),
                        waits=[("DVE", sc), ("DVE", cons_hc[hb])], inc="POOL")
                    yred = S("vector", lambda e, q=q, hb=hb, hs=hs: e.tensor_reduce(
                        yv[q][:, hs], hc[hb][:], AX.X, ALU.add), waits=[("POOL", hmul)], inc="DVE")
                    cons_hc[hb] = yred
                cons_h[q] = hmul
                ldr = pb.dma("scalar", xr[q][:],
                             x_fm.rearrange("(dc p) t -> p dc t", p=128)[:, dc, tslc],
                             f"dR{q}", waits=[("DVE", cons_xr[q])])
                S("vector", lambda e, p=p, dc=dc, q=q, ldr=ldr: e.scalar_tensor_tensor(
                    x1w[:, dc, :], ln0w[p][:, dc, :], Dcol_s[:, dc:dc + 1], xr[q][:],
                    ALU.mult, ALU.add),
                    waits=[(f"dR{q}", ldr), ("dOx", dOx_before)] if dc < 2 else [(f"dR{q}", ldr)])
                xd = S("vector", lambda e, dc=dc, q=q: e.tensor_tensor(
                    x1w[:, dc, :], x1w[:, dc, :], yv[q][:], ALU.add), inc="DVE")
                cons_xr[q] = xd
                pb.dma("sync", x1.rearrange("(dc p) t -> p dc t", p=128)[:, dc, tslc],
                       x1w[:, dc, :], "dOx", waits=[("DVE", xd)])
                return xd

            if pending_tail is not None:
                newt = tail
                x1_done = pending_tail()
                pending_tail = newt
            else:
                pending_tail = tail
        x1_done = pending_tail()
        pending_tail = None
        hmul_last = max(x for x in cons_h if x is not None)
        x1_all = x1_done
        scan_last = pb.cnt["DVE"]
        free_dt = pb.cnt["ACT"]
        free_dt_w = pb.cnt["POOL"]
        free_ln0[p] = pb.cnt["POOL"]
        dOx_before = pb.cnt["dOx"]

        st1 = None
        for dc in range(DC):
            r = dc % 2
            sq1 = S("scalar", lambda e, dc=dc, r=r: e.activation(
                sqr[r][:], x1w[:, dc, :], AF.Square),
                waits=[("PE", cons_sq[r]), ("DVE", x1_all)], inc="ACT")
            wts = [("ACT", sq1)] + ([("DVE", row_done)] if dc == 0 else [])
            S("tensor", lambda e, dc=dc: e.matmul(
                ps_sum[0:1, :], oKf[:], x1w[:, dc, :], start=(dc == 0), stop=(dc == DC - 1)),
              waits=wts)
            st1 = S("tensor", lambda e, r=r, dc=dc: e.matmul(
                ps_sq[0:1, :], oKb[:], sqr[r][:], start=(dc == 0), stop=(dc == DC - 1)),
                inc="PE")
            cons_sq[r] = st1
        S("vector", lambda e, p=p: e.tensor_scalar(rowm[:, p, :], ps_sum[0:1, :], 1.0 / D, None, ALU.mult),
          waits=[("PE", st1)])
        S("vector", lambda e, p=p: e.tensor_scalar(rowq[:, p, :], ps_sq[0:1, :], 1.0 / D, None, ALU.mult))
        S("vector", lambda e, p=p: e.tensor_tensor(rowb[:, p, :], rowm[:, p, :], rowm[:, p, :], ALU.mult))
        var1 = S("vector", lambda e, p=p: e.tensor_tensor(rowq[:, p, :], rowq[:, p, :], rowb[:, p, :], ALU.subtract),
                 inc="DVE")
        S("scalar", lambda e, p=p: e.activation(rowq[:, p, :], rowq[:, p, :], AF.Ln, bias=cbias[0:1, 0:1]),
          waits=[("DVE", var1)])
        rstd1 = S("scalar", lambda e, p=p: e.activation(rowr[:, p, :], rowq[:, p, :], AF.Exp, scale=-0.5),
                  inc="ACT")
        row_done = S("vector", lambda e, p=p: e.scalar_tensor_tensor(
            rowb[:, p, :], rowm[:, p, :], -1.0, rowr[:, p, :], ALU.mult, ALU.mult),
            waits=[("ACT", rstd1)], inc="DVE")
        S("tensor", lambda e, p=p: e.matmul(ps_a[:], o1f[:], rowr[:, p, :], start=True, stop=True),
          waits=[("DVE", row_done), ("ACT", bank_ab_free)])
        bc1 = S("tensor", lambda e, p=p: e.matmul(ps_b[:], o1f[:], rowb[:, p, :], start=True, stop=True),
                inc="PE")
        S("scalar", lambda e: e.copy(ab_bc[:, 0, :], ps_a[:]),
          waits=[("PE", bc1), ("DVE", ln1_done)])
        ab1 = S("scalar", lambda e: e.copy(ab_bc[:, 1, :], ps_b[:]), inc="ACT")
        bank_ab_free = ab1
        for dc in range(DC):
            r = dc % 2
            wts = [("ACT", ab1)] if dc == 0 else []
            S("vector", lambda e, dc=dc, r=r: e.tensor_tensor(
                ln1r[r][:], x1w[:, dc, :], ab_bc[:, 0, :], ALU.mult),
              waits=wts + [("dOl", cons_ln1r[r])])
            ln1_done = S("vector", lambda e, dc=dc, r=r: e.tensor_tensor(
                ln1r[r][:], ln1r[r][:], ab_bc[:, 1, :], ALU.add), inc="DVE")
            pb.dma("sync", ln1b.rearrange("(dc p) t -> p dc t", p=128)[:, dc, tslc],
                   ln1r[r][:], "dOl", waits=[("DVE", ln1_done)])
            cons_ln1r[r] = pb.cnt["dOl"]

    fin = [("dOx", pb.cnt["dOx"]), ("dOl", pb.cnt["dOl"])]
    if DEBUG:
        fin.append(("dDB", pb.cnt["dDB"]))
    S("sync", None, waits=fin)




HT = TPC // 2
NTT = HT // CH
FQ = 896
NFC = FQ // 128
NQ = 4
NE = 8
NBLK = HT // 128


def emit_phase_B(pb, nc, st, io):
    E = st.enter_context
    x1, ln1b, x2, ln2b = io["x1"], io["ln1b"], io["x2"], io["ln2b"]
    w1c, w3c, w2c = io["bW1C"], io["bW3C"], io["bW2C"]

    xac = E(nc.sbuf_tensor("bXA", [128, DC, HT], F32))
    lnr = E(nc.sbuf_tensor("bLN", [128, DC, HT], BF16))
    w1b = [E(nc.sbuf_tensor(f"bW1{j}", [128, DC, FQ], BF16)) for j in range(2)]
    w3b = [E(nc.sbuf_tensor(f"bW3{j}", [128, DC, FQ], BF16)) for j in range(2)]
    w2b = E(nc.sbuf_tensor("bW2", [128, NFC, 1024], BF16))
    interb = E(nc.sbuf_tensor("bIT", [128, NFC, CH], BF16))
    gb = [E(nc.sbuf_tensor(f"bGB{j}", [128, HT], BF16)) for j in range(2)]
    ssb = [E(nc.sbuf_tensor(f"bSS{j}", [128, CH], BF16)) for j in range(2)]
    gw_s = E(nc.sbuf_tensor("bGW", [128, DC, NE], BF16))
    cIdF = E(nc.sbuf_tensor("bIdF", [128, 128], F32))
    E8b = E(nc.sbuf_tensor("bE8", [8, 8 * 128], BF16))
    oKf = E(nc.sbuf_tensor("bOKf", [128, 1], F32))
    o1f = E(nc.sbuf_tensor("bO1f", [1, 128], F32))
    cbias = E(nc.sbuf_tensor("bCB", [128, 2], F32))
    scw = E(nc.sbuf_tensor("bSC", [8, CH], F32))
    sctm = E(nc.sbuf_tensor("bSCT", [128, NBLK, 8], F32))
    gtm = E(nc.sbuf_tensor("bGTM", [128, NBLK, 8], F32))
    gfm = E(nc.sbuf_tensor("bGFM", [8, HT], BF16))
    rt8 = E(nc.sbuf_tensor("bRT8", [128, 10, 8], F32))
    rtc = E(nc.sbuf_tensor("bRTC", [128, 12], F32))
    sqr = [E(nc.sbuf_tensor(f"bSQ{j}", [128, CH], F32)) for j in range(2)]
    rows = E(nc.sbuf_tensor("bROW", [1, 2, CH], F32))
    ab2 = E(nc.sbuf_tensor("bAB", [128, 2, CH], F32))
    ln2r = [E(nc.sbuf_tensor(f"bLN2{j}", [128, CH], BF16)) for j in range(2)]

    pA = [E(nc.psum_tensor(f"bpA{j}", [128, CH], F32)) for j in range(2)]
    pB = [E(nc.psum_tensor(f"bpB{j}", [128, CH], F32)) for j in range(2)]
    pC = [E(nc.psum_tensor(f"bpC{j}", [128, CH], F32)) for j in range(4)]

    S = pb.rec
    if DEBUG:
        dbgB = {k: io[k] for k in ("dbgB_w", "dbgB_gfm", "dbgB_sct", "dbgB_xa",
                                   "dbgB_rt8", "dbgB_rtc")}
    bkA = [None, None]
    bkB = [None, None]
    bkC = [None, None, None, None]

    def bw(t):
        return [t] if t is not None else []

    if getattr(pb, "meta", {}).get("castB") is None:
        for ex in range(NE):
            pb.dma("gpsimd", w1c[ex], io["bW1T"][ex], "dWC")
            pb.dma("gpsimd", w3c[ex], io["bW3T"][ex], "dWC")
            pb.dma("gpsimd", w2c[ex], io["bW2T"][ex], "dWC")
        cast_done = pb.cnt["dWC"]
    else:
        cast_done = pb.meta["castB"]

    pb.dma("sync", cIdF[:], io["cIdentF"], "dC2")
    pb.dma("sync", E8b[:], io["cE8"], "dC2")
    pb.dma("sync", oKf[:], io["cOnesColF"], "dC2")
    pb.dma("sync", o1f[:], io["cOnesRowF"], "dC2")
    pb.dma("sync", cbias[:], io["cBias"], "dC2")
    pb.dma("gpsimd", gw_s[:], io["bGwT"].rearrange("(dc p) m -> p dc m", p=128), "dC2")
    setup = pb.cnt["dC2"]

    ln2_done = None
    row_done = None
    cons_sq = [None, None]
    cons_w13 = [None, None]
    cons_w2 = None
    cons_int = [None, None]
    cons_gb = [None, None]
    cons_ss = [None, None]
    cons_ln2r = [None, None]
    stores_prev = 0
    xac_free_d = None
    xac_free_a = None
    lnr_free = None

    def do_routing(H, ldl):
        nonlocal gt_done
        tr_evac = None
        sc_evac = None
        for tt in range(NTT):
            wts = [("dL", ldl), ("dC2", setup)] + bw(bkC[0])
            if tt > 0:
                wts.append(("ACT", sc_evac))
            for k in range(DC):
                sc_pe = S("tensor", lambda e, k=k, tt=tt: e.matmul(
                    pC[0][0:8, :], gw_s[:, k, :], lnr[:, k, tt * CH:(tt + 1) * CH],
                    start=(k == 0), stop=(k == DC - 1)),
                    waits=wts if k == 0 else (), inc="PE" if k == DC - 1 else None)
            sc_evac = S("scalar", lambda e: e.copy(scw[:], pC[0][0:8, :]),
                        waits=[("PE", sc_pe)], inc="ACT")
            bkC[0] = ("ACT", sc_evac)
            for b4 in range(4):
                blk = tt * 4 + b4
                tp = S("tensor", lambda e, b4=b4: e.transpose(
                    pC[1][:, 0:8], scw[:, b4 * 128:(b4 + 1) * 128], cIdF[0:8, 0:8]),
                    waits=[("ACT", sc_evac)] + bw(bkC[1]), inc="PE")
                tr_evac = S("scalar", lambda e, blk=blk: e.copy(sctm[:, blk, :], pC[1][:, 0:8]),
                            waits=[("PE", tp)], inc="ACT")
                bkC[1] = ("ACT", tr_evac)

        pb.sem("DVE")

        def SV(fn, waits=()):
            w = list(waits) + [("DVE", pb.cnt["DVE"])]
            return S("vector", fn, waits=w, inc="DVE")

        g_done = None
        for blk in range(NBLK):
            s_ = sctm[:, blk, :]
            t1c, nt1, s1c, n1c, r1c, m1c = (rtc[:, i:i + 1] for i in range(6))
            t2c, nt2, s2c, n2c, r2c, m2c = (rtc[:, i:i + 1] for i in range(6, 12))
            a1, f1, dd, oh0, z1, e1, oh1, z2, mk1, mk2 = (rt8[:, i, :] for i in range(10))
            wts = [("ACT", tr_evac)] if blk == 0 else []
            if DEBUG and H == 0 and blk == 1:
                wts = wts + [("dDB", dbg_gate)]
            SV(lambda e, s_=s_, t1c=t1c: e.reduce_max(t1c, s_, axis=AX.X), waits=wts)
            ab_act = S("scalar", lambda e, s_=s_, a1=a1: e.activation(a1, s_, AF.Abs),
                       waits=[("ACT", tr_evac)] + ([("DVE", g_done)] if g_done else []),
                       inc="ACT")
            SV(lambda e, s_=s_, t1c=t1c, oh0=oh0: e.tensor_scalar(oh0, s_, t1c, None, ALU.is_equal))
            SV(lambda e, a1=a1, t1c=t1c, f1=f1: e.tensor_scalar(f1, a1, t1c, None, ALU.max),
              waits=[("ACT", ab_act)])
            SV(lambda e, s_=s_, t1c=t1c, dd=dd: e.tensor_scalar(dd, s_, t1c, None, ALU.subtract))
            SV(lambda e, f1=f1: e.tensor_scalar(f1, f1, -0.02, None, ALU.mult))
            SV(lambda e, dd=dd, f1=f1, mk1=mk1: e.tensor_tensor(mk1, dd, f1, ALU.is_lt))
            SV(lambda e, mk1=mk1, s_=s_, z1=z1: e.scalar_tensor_tensor(
                z1, mk1, -1.0e30, s_, ALU.mult, ALU.add))
            nt1d = SV(lambda e, t1c=t1c, nt1=nt1: e.tensor_scalar(nt1, t1c, -1.0, None, ALU.mult))
            sm = S("scalar", lambda e, z1=z1, e1=e1, nt1=nt1: e.activation(e1, z1, AF.Exp, bias=nt1),
                   waits=[("DVE", nt1d)], inc="ACT")
            SV(lambda e, e1=e1, s1c=s1c: e.reduce_sum(s1c, e1, axis=AX.X),
              waits=[("ACT", sm)])
            SV(lambda e, e1=e1, oh0=oh0: e.tensor_tensor(e1, e1, oh0, ALU.mult))
            SV(lambda e, e1=e1, n1c=n1c: e.reduce_sum(n1c, e1, axis=AX.X))
            rc = SV(lambda e, s1c=s1c, r1c=r1c: e.reciprocal(r1c, s1c))
            m1d = SV(lambda e, n1c=n1c, r1c=r1c, m1c=m1c: e.tensor_tensor(m1c, n1c, r1c, ALU.mult))
            SV(lambda e, oh0=oh0, s_=s_, z2=z2: e.scalar_tensor_tensor(
                z2, oh0, -1.0e30, s_, ALU.mult, ALU.add))
            SV(lambda e, z2=z2, t2c=t2c: e.reduce_max(t2c, z2, axis=AX.X))
            SV(lambda e, z2=z2, t2c=t2c, oh1=oh1: e.tensor_scalar(oh1, z2, t2c, None, ALU.is_equal))
            SV(lambda e, a1=a1, t2c=t2c, f1=f1: e.tensor_scalar(f1, a1, t2c, None, ALU.max))
            SV(lambda e, s_=s_, t2c=t2c, dd=dd: e.tensor_scalar(dd, s_, t2c, None, ALU.subtract))
            SV(lambda e, f1=f1: e.tensor_scalar(f1, f1, -0.02, None, ALU.mult))
            SV(lambda e, dd=dd, f1=f1, mk2=mk2: e.tensor_tensor(mk2, dd, f1, ALU.is_lt))
            SV(lambda e, mk2=mk2, z2=z2: e.scalar_tensor_tensor(
                z2, mk2, -1.0e30, z2, ALU.mult, ALU.add))
            nt2d = SV(lambda e, t2c=t2c, nt2=nt2: e.tensor_scalar(nt2, t2c, -1.0, None, ALU.mult))
            sm2 = S("scalar", lambda e, z2=z2, nt2=nt2: e.activation(z2, z2, AF.Exp, bias=nt2),
                    waits=[("DVE", nt2d)], inc="ACT")
            SV(lambda e, z2=z2, s2c=s2c: e.reduce_sum(s2c, z2, axis=AX.X),
              waits=[("ACT", sm2)])
            SV(lambda e, z2=z2, oh1=oh1: e.tensor_tensor(z2, z2, oh1, ALU.mult))
            SV(lambda e, z2=z2, n2c=n2c: e.reduce_sum(n2c, z2, axis=AX.X))
            rc2 = SV(lambda e, s2c=s2c, r2c=r2c: e.reciprocal(r2c, s2c))
            m2d = SV(lambda e, n2c=n2c, r2c=r2c, m2c=m2c: e.tensor_tensor(m2c, n2c, r2c, ALU.mult))
            SV(lambda e, oh0=oh0, m1c=m1c: e.tensor_scalar(oh0, oh0, m1c, None, ALU.mult))
            g_done = SV(lambda e, oh1=oh1, m2c=m2c, oh0=oh0, blk=blk: e.scalar_tensor_tensor(
                gtm[:, blk, :], oh1, m2c, oh0, ALU.mult, ALU.add))
            if DEBUG and H == 0 and blk == 0:
                pb.dma("sync", dbgB["dbgB_rt8"], rt8[:], "dDB", waits=[("DVE", g_done)])
                pb.dma("sync", dbgB["dbgB_rtc"], rtc[:], "dDB")
                dbg_gate = pb.cnt["dDB"]

        gt_evac = None
        for blk in range(NBLK):
            wts = [("DVE", g_done)] + bw(bkC[1])
            tp = S("tensor", lambda e, blk=blk: e.transpose(
                pC[1][0:8, 0:128], gtm[:, blk, :], cIdF[:]),
                waits=wts, inc="PE")
            gt_evac = S("scalar", lambda e, blk=blk: e.copy(
                gfm[:, blk * 128:(blk + 1) * 128], pC[1][0:8, 0:128]),
                waits=[("PE", tp)], inc="ACT")
            bkC[1] = ("ACT", gt_evac)

        if DEBUG and H == 0:
            pb.dma("sync", dbgB["dbgB_gfm"], gfm[:], "dDB", waits=[("ACT", gt_evac)])
            pb.dma("sync", dbgB["dbgB_sct"], gtm[:], "dDB", waits=[("DVE", g_done)])
            pb.dma("sync", dbgB["dbgB_xa"], xac[:], "dDB", waits=[("dXB", ldx)])
        gt_done = gt_evac

    def do_ln2(H, hsl, ev):
        nonlocal row_done, ln2_done, stores_prev, xac_free_d, xac_free_a
        ev_all = ev
        for tt in range(NTT):
            tsl = slice(tt * CH, (tt + 1) * CH)
            st2 = None
            for dc in range(DC):
                r = dc % 2
                sq1 = S("scalar", lambda e, dc=dc, r=r, tsl=tsl: e.activation(
                    sqr[r][:], xac[:, dc, tsl], AF.Square),
                    waits=[("PE", cons_sq[r]), ("DVE", ev_all), ("DVE", row_done)], inc="ACT")
                wts = [("ACT", sq1)] + ([("DVE", row_done)] if dc == 0 else [])
                if dc == 0:
                    wts += bw(bkA[1]) + bw(bkB[1])
                S("tensor", lambda e, dc=dc, tsl=tsl: e.matmul(
                    pA[1][0:1, :], oKf[:], xac[:, dc, tsl], start=(dc == 0), stop=(dc == DC - 1)),
                  waits=wts)
                st2 = S("tensor", lambda e, r=r, dc=dc: e.matmul(
                    pB[1][0:1, :], oKf[:], sqr[r][:], start=(dc == 0), stop=(dc == DC - 1)),
                    inc="PE")
                cons_sq[r] = st2
            S("vector", lambda e: e.tensor_scalar(rows[:, 0, :], pA[1][0:1, :], 1.0 / D, None, ALU.mult),
              waits=[("PE", st2)])
            S("vector", lambda e: e.tensor_scalar(rows[:, 1, :], pB[1][0:1, :], 1.0 / D, None, ALU.mult))
            S("vector", lambda e: e.tensor_tensor(sqr[0][0:1, 0:CH], rows[:, 0, :], rows[:, 0, :], ALU.mult))
            var_d = S("vector", lambda e: e.tensor_tensor(rows[:, 1, :], rows[:, 1, :], sqr[0][0:1, 0:CH], ALU.subtract),
                      inc="DVE")
            bkA[1] = ("DVE", var_d)
            bkB[1] = ("DVE", var_d)
            S("scalar", lambda e: e.activation(rows[:, 1, :], rows[:, 1, :], AF.Ln, bias=cbias[0:1, 0:1]),
              waits=[("DVE", var_d)])
            rstd_a = S("scalar", lambda e: e.activation(rows[:, 1, :], rows[:, 1, :], AF.Exp, scale=-0.5),
                       inc="ACT")
            row_done = S("vector", lambda e: e.scalar_tensor_tensor(
                rows[:, 0, :], rows[:, 0, :], -1.0, rows[:, 1, :], ALU.mult, ALU.mult),
                waits=[("ACT", rstd_a)], inc="DVE")
            S("tensor", lambda e: e.matmul(pA[1][:], o1f[:], rows[:, 1, :], start=True, stop=True),
              waits=[("DVE", row_done)] + bw(bkA[1]) + bw(bkB[1]))
            bc1 = S("tensor", lambda e: e.matmul(pB[1][:], o1f[:], rows[:, 0, :], start=True, stop=True),
                    inc="PE")
            S("scalar", lambda e: e.copy(ab2[:, 0, :], pA[1][:]),
              waits=[("PE", bc1), ("DVE", ln2_done)])
            ab_done = S("scalar", lambda e: e.copy(ab2[:, 1, :], pB[1][:]), inc="ACT")
            bkA[1] = ("ACT", ab_done)
            bkB[1] = ("ACT", ab_done)
            for dc in range(DC):
                r = dc % 2
                wts = [("ACT", ab_done)] if dc == 0 else []
                S("vector", lambda e, dc=dc, r=r, tsl=tsl: e.tensor_tensor(
                    ln2r[r][:], xac[:, dc, tsl], ab2[:, 0, :], ALU.mult),
                  waits=wts + [("dOl2", cons_ln2r[r])])
                ln2_done = S("vector", lambda e, r=r: e.tensor_tensor(
                    ln2r[r][:], ln2r[r][:], ab2[:, 1, :], ALU.add), inc="DVE")
                pb.dma("sync", ln2b.rearrange("(dc p) t -> p dc t", p=128)[:, dc, hsl][:, tsl],
                       ln2r[r][:], "dOl2", waits=[("DVE", ln2_done)])
                cons_ln2r[r] = pb.cnt["dOl2"]
        pb.dma("sync", x2.rearrange("(dc p) t -> p dc t", p=128)[:, :, hsl], xac[:],
               "dOx2", waits=[("DVE", ln2_done)])
        stores_prev = pb.cnt["dOx2"]
        xac_free_d = pb.cnt["DVE"]
        xac_free_a = pb.cnt["ACT"]
        lnr_free = pb.cnt["PE"]


    gt_done = None
    ldl0 = pb.dma("sync", lnr[:],
                  ln1b.rearrange("(dc p) t -> p dc t", p=128)[:, :, 0:HT], "dL",
                  waits=[("dC2", setup)])
    do_routing(0, ldl0)
    ldl_cur = ldl0
    for H in range(2):
        hsl = slice(H * HT, (H + 1) * HT)
        ldl = ldl_cur
        gt_evac = gt_done
        ldx = pb.dma("sync", xac[:],
                     x1.rearrange("(dc p) t -> p dc t", p=128)[:, :, hsl], "dXB",
                     waits=[("dC2", setup), ("DVE", xac_free_d), ("ACT", xac_free_a),
                            ("dOx2", stores_prev)])
        for ex in range(NE):
            j = ex % 2
            gb_evac = None
            for tt in range(NTT):
                wts = [("ACT", gt_evac), ("POOL", cons_gb[j])] + bw(bkA[0])
                gp = S("tensor", lambda e, ex=ex, tt=tt: e.matmul(
                    pA[0][:], E8b[:, ex * 128:(ex + 1) * 128], gfm[:, tt * CH:(tt + 1) * CH],
                    start=True, stop=True), waits=wts, inc="PE")
                gb_evac = S("scalar", lambda e, j=j, tt=tt: e.copy(
                    gb[j][:, tt * CH:(tt + 1) * CH], pA[0][:]),
                    waits=[("PE", gp)], inc="ACT")
                bkA[0] = ("ACT", gb_evac)

            for q in range(NQ):
                qp = q % 2
                fsl = slice(q * FQ, (q + 1) * FQ)
                pb.dma("sync", w1b[qp][:],
                       w1c[ex].rearrange("(kc p) f -> p kc f", p=128)[:, :, fsl],
                       f"dW{qp}", waits=[("dWC", cast_done), ("PE", cons_w13[qp])])
                pb.dma("sync", w3b[qp][:],
                       w3c[ex].rearrange("(kc p) f -> p kc f", p=128)[:, :, fsl],
                       f"dW{qp}")
                w13_ld = pb.cnt[f"dW{qp}"]
                ldw = pb.dma("sync", w2b[:],
                             w2c[ex].rearrange("(fc p) h -> p fc h", p=128)[:, q * NFC:(q + 1) * NFC, :],
                             f"dW{qp}", waits=[("PE", cons_w2)])

                if DEBUG and H == 0 and ex == 0 and q == 0:
                    pb.dma("sync", dbgB["dbgB_w"], w1b[qp][:], "dDB",
                           waits=[(f"dW{qp}", w13_ld)])
                gm = None
                mmw = None
                ev = None
                gm_done = [None] * NFC
                for tt in range(NTT):
                    ttp = tt % 2
                    tsl = slice(tt * CH, (tt + 1) * CH)
                    for fc in range(NFC):
                        fp = fc % 2
                        wts = [(f"dW{qp}", w13_ld), ("dL", ldl)] + bw(bkA[fp]) + bw(bkB[fp])
                        for k in range(DC):
                            S("tensor", lambda e, j2=qp, k=k, fc=fc, fp=fp, tsl=tsl: e.matmul(
                                pA[fp][:], w1b[j2][:, k, fc * 128:(fc + 1) * 128],
                                lnr[:, k, tsl], start=(k == 0), stop=(k == DC - 1)),
                              waits=wts if k == 0 else ())
                            h_pe = S("tensor", lambda e, j2=qp, k=k, fc=fc, fp=fp, tsl=tsl: e.matmul(
                                pB[fp][:], w3b[j2][:, k, fc * 128:(fc + 1) * 128],
                                lnr[:, k, tsl], start=(k == 0), stop=(k == DC - 1)),
                                inc="PE" if k == DC - 1 else None)
                        sl = S("scalar", lambda e, fp=fp: e.activation(
                            ssb[fp][:], pA[fp][:], AF.Silu),
                            waits=[("PE", h_pe), ("DVE", cons_ss[fp])], inc="ACT")
                        bkA[fp] = ("ACT", sl)
                        it = S("vector", lambda e, fp=fp, fc=fc: e.tensor_tensor(
                            interb[:, fc, :], ssb[fp][:], pB[fp][:], ALU.mult),
                            waits=[("ACT", sl), ("PE", cons_int[0])], inc="DVE")
                        cons_ss[fp] = it
                        bkB[fp] = ("DVE", it)
                        gm = S("gpsimd", lambda e, j=j, fc=fc, tsl=tsl: e.tensor_tensor(
                            interb[:, fc, :], interb[:, fc, :],
                            gb[j][:, tsl], ALU.mult),
                            waits=[("DVE", it), ("ACT", gb_evac)], inc="POOL")
                        gm_done[fc] = gm
                    for hc in range(DC):
                        cb = hc % 4
                        wts0 = [(f"dW{qp}", ldw)] + bw(bkC[cb])
                        for fc in range(NFC):
                            wts = list(wts0 if fc == 0 else []) + (
                                [("POOL", gm_done[fc])] if hc == 0 else [])
                            mmw = S("tensor", lambda e, fc=fc, hc=hc, cb=cb: e.matmul(
                                pC[cb][:], w2b[:, fc, hc * 128:(hc + 1) * 128],
                                interb[:, fc, :], start=(fc == 0), stop=(fc == NFC - 1)),
                                waits=wts,
                                inc="PE" if fc == NFC - 1 else None)
                        cons_int[0] = mmw
                        ev = S("vector", lambda e, hc=hc, cb=cb, tsl=tsl: e.tensor_tensor(
                            xac[:, hc, tsl], xac[:, hc, tsl], pC[cb][:], ALU.add),
                            waits=[("PE", mmw), ("dXB", ldx)], inc="DVE")
                        bkC[cb] = ("DVE", ev)
                cons_w13[qp] = pb.cnt["PE"]
                cons_w2 = pb.cnt["PE"]
            cons_gb[j] = pb.cnt["POOL"]

        if H == 0:
            lnr_free = pb.cnt["PE"]
            ldl_cur = pb.dma("sync", lnr[:],
                             ln1b.rearrange("(dc p) t -> p dc t", p=128)[:, :, HT:TPC], "dL",
                             waits=[("PE", lnr_free)])
            do_routing(1, ldl_cur)
        do_ln2(H, hsl, ev)
    S("sync", None, waits=[("dOx2", pb.cnt["dOx2"]), ("dOl2", pb.cnt["dOl2"])])




def emit_phase_C(pb, nc, st, io):
    E = st.enter_context
    x2, ln2b, x3, ln3b = io["x2"], io["ln2b"], io["x3"], io["ln3b"]

    ln2r = E(nc.sbuf_tensor("cLN2", [128, DC, TPC], BF16))
    Bw_s = E(nc.sbuf_tensor("cBW", [128, DC, 256], BF16))
    Cw_s = E(nc.sbuf_tensor("cCW", [128, 2, DC, 128], BF16))
    lamc = E(nc.sbuf_tensor("cLamC", [128, 2], F32))
    lam_t = E(nc.sbuf_tensor("cLamT", [128, 2, SEQ], F32))
    Dc_s = E(nc.sbuf_tensor("cDS", [128, DC], F32))
    Bu_sb = E(nc.sbuf_tensor("cBU", [128, 2, TPC], F32))
    hb = E(nc.sbuf_tensor("cHB", [128, 2, TPC], BF16))
    x3w = [E(nc.sbuf_tensor(f"cX3{j}", [128, DC, CH], F32)) for j in range(2)]
    x2r = [E(nc.sbuf_tensor(f"cX2{j}", [128, CH], F32)) for j in range(2)]
    oKf = E(nc.sbuf_tensor("cOKf", [128, 1], F32))
    o1f = E(nc.sbuf_tensor("cO1f", [1, 128], F32))
    cbias = E(nc.sbuf_tensor("cCB", [128, 2], F32))
    sqr = [E(nc.sbuf_tensor(f"cSQ{j}", [128, CH], F32)) for j in range(2)]
    rows = E(nc.sbuf_tensor("cROW", [1, 2, CH], F32))
    ab3 = E(nc.sbuf_tensor("cAB", [128, 2, CH], F32))
    ln3r = [E(nc.sbuf_tensor(f"cLN3{j}", [128, CH], BF16)) for j in range(2)]

    pY = [E(nc.psum_tensor(f"cpY{j}", [128, CH], F32)) for j in range(2)]
    pS = [E(nc.psum_tensor(f"cpS{j}", [128, CH], F32)) for j in range(2)]
    pBU = [E(nc.psum_tensor(f"cpB{j}", [128, CH], F32)) for j in range(2)]

    S = pb.rec

    pb.dma("sync", oKf[:], io["cOnesColF"], "dC3")
    pb.dma("sync", o1f[:], io["cOnesRowF"], "dC3")
    pb.dma("sync", cbias[:], io["cBias"], "dC3")
    pb.dma("sync", Dc_s[:], io["cD1"], "dC3")
    pb.dma("sync", lamc[:], io["cLogLam"], "dC3")
    pb.dma("gpsimd", Bw_s[:], io["cBwT"].rearrange("(dc p) n -> p dc n", p=128), "dG3")
    pb.dma("gpsimd", Cw_s[:], io["cCwT"].rearrange("(nc p) (dc m) -> p nc dc m", p=128, m=128), "dG3")
    ldl2 = pb.dma("sync", ln2r[:], ln2b.rearrange("(dc p) t -> p dc t", p=128), "dC3")
    setup = pb.cnt["dC3"]
    setup_g = pb.cnt["dG3"]
    sig = S("scalar", lambda e: e.activation(lamc[:], lamc[:], AF.Sigmoid),
            waits=[("dC3", setup)], inc="ACT")
    ms = S("gpsimd", lambda e: e.memset(lam_t[:], 1.0), inc="POOL")
    lam_d = S("vector", lambda e: e.tensor_scalar(lam_t[:, 0, :], lam_t[:, 0, :], lamc[:, 0:1], None, ALU.mult),
              waits=[("ACT", sig), ("POOL", ms)], inc="DVE")
    lam_d = S("vector", lambda e: e.tensor_scalar(lam_t[:, 1, :], lam_t[:, 1, :], lamc[:, 1:2], None, ALU.mult),
              waits=[("DVE", lam_d)], inc="DVE")

    bu_evac = None
    scans = []
    for b in range(2):
        for nc2 in range(2):
            for tt in range(4):
                tg = b * SEQ + tt * CH
                jp = tt % 2
                wts = [("dC3", setup), ("dG3", setup_g)]
                if bu_evac is not None and (b + nc2 + tt):
                    wts.append(("DVE", bu_evac))
                for k in range(DC):
                    bu_pe = S("tensor", lambda e, k=k, nc2=nc2, tg=tg, jp=jp: e.matmul(
                        pBU[jp][:], Bw_s[:, k, nc2 * 128:(nc2 + 1) * 128],
                        ln2r[:, k, tg:tg + CH], start=(k == 0), stop=(k == DC - 1)),
                        waits=wts if k == 0 else (), inc="PE" if k == DC - 1 else None)
                bu_evac = S("vector", lambda e, nc2=nc2, tg=tg, jp=jp: e.tensor_copy(
                    Bu_sb[:, nc2, tg:tg + CH], pBU[jp][:]),
                    waits=[("PE", bu_pe)], inc="DVE")
            sc = S("vector", lambda e, b=b, nc2=nc2: e.tensor_tensor_scan(
                hb[:, nc2, b * SEQ:(b + 1) * SEQ], lam_t[:, nc2, :],
                Bu_sb[:, nc2, b * SEQ:(b + 1) * SEQ], 0.0, ALU.mult, ALU.add),
                waits=[("DVE", bu_evac), ("DVE", lam_d)], inc="DVE")
            scans.append(sc)
    scans_done = scans[-1]

    row_done = None
    ln3_done = None
    cons_sq = [None, None]
    cons_x2 = [None, None]
    cons_ln3r = [None, None]
    bankY = [None, None]
    bankS = [None, None]
    stp = 0
    for tt in range(NCH):
        p = tt % 2
        tsl = slice(tt * CH, (tt + 1) * CH)
        for dc in range(DC):
            jp = dc % 2
            wts = [("DVE", scans_done)] + ([bankY[jp]] if bankY[jp] else [])
            for nc2 in range(2):
                y_pe = S("tensor", lambda e, nc2=nc2, dc=dc, tsl=tsl, jp=jp: e.matmul(
                    pY[jp][:], Cw_s[:, nc2, dc, :], hb[:, nc2, tsl],
                    start=(nc2 == 0), stop=(nc2 == 1)),
                    waits=wts if nc2 == 0 else (), inc="PE" if nc2 == 1 else None)
            ldx2 = pb.dma("scalar", x2r[jp][:],
                          x2.rearrange("(dc p) t -> p dc t", p=128)[:, dc, tsl],
                          f"dX2{jp}", waits=[("DVE", cons_x2[jp])])
            S("vector", lambda e, p=p, dc=dc, jp=jp, tsl=tsl, ldx2=ldx2: e.scalar_tensor_tensor(
                x3w[p][:, dc, :], ln2r[:, dc, tsl], Dc_s[:, dc:dc + 1], x2r[jp][:],
                ALU.mult, ALU.add),
                waits=[(f"dX2{jp}", ldx2), ("dOx3", stp)] if dc == 0 else [(f"dX2{jp}", ldx2)])
            x3d = S("vector", lambda e, p=p, dc=dc, jp=jp: e.tensor_tensor(
                x3w[p][:, dc, :], x3w[p][:, dc, :], pY[jp][:], ALU.add),
                waits=[("PE", y_pe)], inc="DVE")
            cons_x2[jp] = x3d
            bankY[jp] = ("DVE", x3d)
        x3_all = x3d
        st3 = None
        for dc in range(DC):
            r = dc % 2
            sq1 = S("scalar", lambda e, p=p, dc=dc, r=r: e.activation(
                sqr[r][:], x3w[p][:, dc, :], AF.Square),
                waits=[("PE", cons_sq[r]), ("DVE", x3_all), ("DVE", row_done)], inc="ACT")
            wts = [("ACT", sq1)] + ([("DVE", row_done)] if dc == 0 else [])
            if dc == 0:
                wts += [bankS[0]] if bankS[0] else []
                wts += [bankS[1]] if bankS[1] else []
            S("tensor", lambda e, p=p, dc=dc: e.matmul(
                pS[0][0:1, :], oKf[:], x3w[p][:, dc, :], start=(dc == 0), stop=(dc == DC - 1)),
              waits=wts)
            st3 = S("tensor", lambda e, r=r, dc=dc: e.matmul(
                pS[1][0:1, :], oKf[:], sqr[r][:], start=(dc == 0), stop=(dc == DC - 1)),
                inc="PE")
            cons_sq[r] = st3
        S("vector", lambda e: e.tensor_scalar(rows[:, 0, :], pS[0][0:1, :], 1.0 / D, None, ALU.mult),
          waits=[("PE", st3)])
        S("vector", lambda e: e.tensor_scalar(rows[:, 1, :], pS[1][0:1, :], 1.0 / D, None, ALU.mult))
        S("vector", lambda e: e.tensor_tensor(sqr[0][0:1, 0:CH], rows[:, 0, :], rows[:, 0, :], ALU.mult))
        var_d = S("vector", lambda e: e.tensor_tensor(rows[:, 1, :], rows[:, 1, :], sqr[0][0:1, 0:CH], ALU.subtract),
                  inc="DVE")
        S("scalar", lambda e: e.activation(rows[:, 1, :], rows[:, 1, :], AF.Ln, bias=cbias[0:1, 0:1]),
          waits=[("DVE", var_d)])
        rstd_a = S("scalar", lambda e: e.activation(rows[:, 1, :], rows[:, 1, :], AF.Exp, scale=-0.5),
                   inc="ACT")
        row_done = S("vector", lambda e: e.scalar_tensor_tensor(
            rows[:, 0, :], rows[:, 0, :], -1.0, rows[:, 1, :], ALU.mult, ALU.mult),
            waits=[("ACT", rstd_a)], inc="DVE")
        S("tensor", lambda e: e.matmul(pS[0][:], o1f[:], rows[:, 1, :], start=True, stop=True),
          waits=[("DVE", row_done)])
        bc3 = S("tensor", lambda e: e.matmul(pS[1][:], o1f[:], rows[:, 0, :], start=True, stop=True),
                inc="PE")
        S("scalar", lambda e: e.copy(ab3[:, 0, :], pS[0][:]),
          waits=[("PE", bc3), ("DVE", ln3_done)])
        ab_d = S("scalar", lambda e: e.copy(ab3[:, 1, :], pS[1][:]), inc="ACT")
        bankS[0] = ("ACT", ab_d)
        bankS[1] = ("ACT", ab_d)
        for dc in range(DC):
            r = dc % 2
            wts = [("ACT", ab_d)] if dc == 0 else []
            S("vector", lambda e, p=p, dc=dc, r=r: e.tensor_tensor(
                ln3r[r][:], x3w[p][:, dc, :], ab3[:, 0, :], ALU.mult),
              waits=wts + [("dOl3", cons_ln3r[r])])
            ln3_done = S("vector", lambda e, r=r: e.tensor_tensor(
                ln3r[r][:], ln3r[r][:], ab3[:, 1, :], ALU.add), inc="DVE")
            pb.dma("sync", ln3b.rearrange("(dc p) t -> p dc t", p=128)[:, dc, tsl],
                   ln3r[r][:], "dOl3", waits=[("DVE", ln3_done)])
            cons_ln3r[r] = pb.cnt["dOl3"]
        pb.dma("sync", x3.rearrange("(dc p) t -> p dc t", p=128)[:, :, tsl], x3w[p][:],
               "dOx3", waits=[("DVE", ln3_done)])
        stp = pb.cnt["dOx3"]

    S("sync", None, waits=[("dOx3", pb.cnt["dOx3"]), ("dOl3", pb.cnt["dOl3"])])



TQ = 1024
NQT = TPC // TQ
FH = 4096
NFH = FH // 128


def emit_phase_D(pb, nc, st, io):
    E = st.enter_context
    x3, ln3b, out_fm = io["x3"], io["ln3b"], io["out_fm"]
    w1C, w2C, h1C, h2C, w3C = io["dW1C"], io["dW2C"], io["dH1C"], io["dH2C"], io["dW3C"]

    h_in = E(nc.sbuf_tensor("dHIN", [128, NFH, TQ], BF16))
    cbuf = E(nc.sbuf_tensor("dCB2", [128, NFH, TQ], BF16))
    lnq = E(nc.sbuf_tensor("dLNQ", [128, DC, TQ], BF16))
    wq1 = [E(nc.sbuf_tensor(f"dWQ1{j}", [128, DC, 512], BF16)) for j in range(2)]
    wq2 = [E(nc.sbuf_tensor(f"dWQ2{j}", [128, DC, 512], BF16)) for j in range(2)]
    hqb = [E(nc.sbuf_tensor(f"dHQ{j}", [128, NFH, 128], BF16)) for j in range(2)]
    ssb = [E(nc.sbuf_tensor(f"dSS{j}", [128, CH], BF16)) for j in range(2)]
    s1c = E(nc.sbuf_tensor("dS1", [128, NFH], F32))
    s2c = E(nc.sbuf_tensor("dS2", [128, NFH], F32))
    hs1c = E(nc.sbuf_tensor("dHS1", [128, NFH], F32))
    hs2c = E(nc.sbuf_tensor("dHS2", [128, NFH], F32))
    s3c = E(nc.sbuf_tensor("dS3", [128, DC], F32))
    x3r = [E(nc.sbuf_tensor(f"dX3{j}", [128, CH], F32)) for j in range(2)]
    outw = [E(nc.sbuf_tensor(f"dOW{j}", [128, CH], F32)) for j in range(2)]

    pA = [E(nc.psum_tensor(f"dpA{j}", [128, CH], F32)) for j in range(2)]
    pB = [E(nc.psum_tensor(f"dpB{j}", [128, CH], F32)) for j in range(2)]
    pD = [E(nc.psum_tensor(f"dpD{j}", [128, CH], F32)) for j in range(4)]

    S = pb.rec

    if getattr(pb, "meta", {}).get("castD") is None:
        pb.dma("gpsimd", w1C, io["dW1T"], "dWC2")
        pb.dma("gpsimd", w2C, io["dW2T"], "dWC2")
        pb.dma("gpsimd", h1C, io["dH1T"], "dWC2")
        pb.dma("gpsimd", h2C, io["dH2T"], "dWC2")
        pb.dma("gpsimd", w3C, io["dW3T"], "dWC2")
        cast_done = pb.cnt["dWC2"]
    else:
        cast_done = pb.meta["castD"]

    pb.dma("sync", s1c[:], io["dS1c"], "dC4")
    pb.dma("sync", s2c[:], io["dS2c"], "dC4")
    pb.dma("sync", hs1c[:], io["dHS1c"], "dC4")
    pb.dma("sync", hs2c[:], io["dHS2c"], "dC4")
    pb.dma("sync", s3c[:], io["dS3c"], "dC4")
    setup = pb.cnt["dC4"]

    cons_w1 = [None] * 2
    cons_hq = [None, None]
    cons_ss = [None, None]
    cons_x3 = [None, None]
    cons_out = [None, None]
    bkA = [None, None]
    bkB = [None, None]
    bkD = [None, None, None, None]
    lnq_free = None
    hin_free = None
    cbuf_free = None

    def bw(t):
        return [t] if t is not None else []

    for Q in range(NQT):
        qsl = slice(Q * TQ, (Q + 1) * TQ)
        ldl = pb.dma("sync", lnq[:],
                     ln3b.rearrange("(dc p) t -> p dc t", p=128)[:, :, qsl], "dLQ",
                     waits=[("dC4", setup), ("PE", lnq_free)])
        it = None
        for fg in range(8):
            j = fg % 2
            gsl = slice(fg * 512, (fg + 1) * 512)
            ldwa = pb.dma("sync", wq1[j][:],
                   w1C.rearrange("(kc p) f -> p kc f", p=128)[:, :, gsl], f"dWD{j}",
                   waits=[("dWC2", cast_done), ("PE", cons_w1[j])])
            ldw = pb.dma("scalar", wq2[j][:],
                         w2C.rearrange("(kc p) f -> p kc f", p=128)[:, :, gsl], f"dWE{j}",
                         waits=[("dWC2", cast_done), ("PE", cons_w1[j])])
            for f4 in range(4):
                fcp = fg * 4 + f4
                for tt in range(2):
                    fp = (fcp * 2 + tt) % 2
                    tsl2 = slice(tt * CH, (tt + 1) * CH)
                    wts = [(f"dWD{j}", ldwa), (f"dWE{j}", ldw), ("dLQ", ldl)] + bw(bkA[fp]) + bw(bkB[fp])
                    for k in range(DC):
                        S("tensor", lambda e, j=j, k=k, f4=f4, fp=fp, tsl2=tsl2: e.matmul(
                            pA[fp][:], wq1[j][:, k, f4 * 128:(f4 + 1) * 128],
                            lnq[:, k, tsl2], start=(k == 0), stop=(k == DC - 1)),
                          waits=wts if k == 0 else ())
                        h_pe = S("tensor", lambda e, j=j, k=k, f4=f4, fp=fp, tsl2=tsl2: e.matmul(
                            pB[fp][:], wq2[j][:, k, f4 * 128:(f4 + 1) * 128],
                            lnq[:, k, tsl2], start=(k == 0), stop=(k == DC - 1)),
                            inc="PE" if k == DC - 1 else None)
                    sl = S("scalar", lambda e, fp=fp, fcp=fcp: e.activation(
                        ssb[fp][:], pA[fp][:], AF.Silu, scale=s1c[:, fcp:fcp + 1]),
                        waits=[("PE", h_pe), ("DVE", cons_ss[fp])], inc="ACT")
                    bkA[fp] = ("ACT", sl)
                    it = S("vector", lambda e, fp=fp, fcp=fcp, tsl2=tsl2: e.scalar_tensor_tensor(
                        h_in[:, fcp, tsl2], pB[fp][:], s2c[:, fcp:fcp + 1], ssb[fp][:],
                        ALU.mult, ALU.mult),
                        waits=[("ACT", sl), ("PE", hin_free)], inc="DVE")
                    cons_ss[fp] = it
                    bkB[fp] = ("DVE", it)
            cons_w1[j] = pb.cnt["PE"]
        s1_done = it

        for stage, (wsrc, scol, src_buf, dst_buf) in enumerate(
                ((h1C, hs1c, h_in, cbuf), (h2C, hs2c, cbuf, h_in))):
            gel = None
            for fcp in range(NFH):
                j = fcp % 2
                ldh = pb.dma("sync", hqb[j][:],
                             wsrc.rearrange("(kc p) f -> p kc f", p=128)[:, :, fcp * 128:(fcp + 1) * 128],
                             f"dWH{j}", waits=[("dWC2", cast_done), ("PE", cons_hq[j])])
                for tt in range(2):
                    fp = (fcp * 2 + tt) % 2
                    tsl2 = slice(tt * CH, (tt + 1) * CH)
                    wts = [(f"dWH{j}", ldh)] + bw(bkA[fp])
                    if stage == 0 and fcp == 0 and tt == 0:
                        wts.append(("DVE", s1_done))
                    if stage == 1 and fcp == 0 and tt == 0:
                        wts.append(("ACT", c_done))
                        wts.append(("DVE", pb.cnt["DVE"]))
                    for k in range(NFH):
                        mm = S("tensor", lambda e, j=j, k=k, fp=fp, tsl2=tsl2, src_buf=src_buf: e.matmul(
                            pA[fp][:], hqb[j][:, k, :], src_buf[:, k, tsl2],
                            start=(k == 0), stop=(k == NFH - 1)),
                            waits=wts if k == 0 else (), inc="PE" if k == NFH - 1 else None)
                    gel = S("scalar", lambda e, fp=fp, fcp=fcp, tt=tt, scol=scol, dst_buf=dst_buf, tsl2=tsl2: e.activation(
                        dst_buf[:, fcp, tsl2], pA[fp][:], AF.Gelu, scale=scol[:, fcp:fcp + 1]),
                        waits=[("PE", mm)], inc="ACT")
                    bkA[fp] = ("ACT", gel)
                cons_hq[j] = pb.cnt["PE"]
            if stage == 0:
                c_done = gel
                hin_free = pb.cnt["PE"]
            else:
                d_done = gel
                cbuf_free = pb.cnt["PE"]

        for dc in range(DC):
            j = dc % 2
            ldw3 = pb.dma("sync", hqb[j][:],
                          w3C.rearrange("(kc p) f -> p kc f", p=128)[:, :, dc * 128:(dc + 1) * 128],
                          f"dWH{j}", waits=[("dWC2", cast_done), ("PE", cons_hq[j])])
            for tt in range(2):
                cb = (dc * 2 + tt) % 4
                tsl2 = slice(tt * CH, (tt + 1) * CH)
                gtsl = slice(Q * TQ + tt * CH, Q * TQ + (tt + 1) * CH)
                wts = [(f"dWH{j}", ldw3)] + bw(bkD[cb])
                if dc == 0 and tt == 0:
                    wts.append(("ACT", d_done))
                for k in range(NFH):
                    mm = S("tensor", lambda e, j=j, k=k, cb=cb, tsl2=tsl2: e.matmul(
                        pD[cb][:], hqb[j][:, k, :], h_in[:, k, tsl2],
                        start=(k == 0), stop=(k == NFH - 1)),
                        waits=wts if k == 0 else (), inc="PE" if k == NFH - 1 else None)
                jo = tt % 2
                ldx3 = pb.dma("scalar", x3r[jo][:],
                              x3.rearrange("(dc p) t -> p dc t", p=128)[:, dc, gtsl],
                              f"dX3{jo}", waits=[("DVE", cons_x3[jo])])
                ov = S("vector", lambda e, cb=cb, jo=jo, dc=dc: e.scalar_tensor_tensor(
                    outw[jo][:], pD[cb][:], s3c[:, dc:dc + 1], x3r[jo][:],
                    ALU.mult, ALU.add),
                    waits=[("PE", mm), (f"dX3{jo}", ldx3), ("dOut", cons_out[jo])], inc="DVE")
                cons_x3[jo] = ov
                bkD[cb] = ("DVE", ov)
                pb.dma("sync", out_fm.rearrange("(dc p) t -> p dc t", p=128)[:, dc, gtsl],
                       outw[jo][:], "dOut", waits=[("DVE", ov)])
                cons_out[jo] = pb.cnt["dOut"]
            cons_hq[j] = pb.cnt["PE"]
        lnq_free = pb.cnt["PE"]
        hin_free = pb.cnt["PE"]

    S("sync", None, waits=[("dOut", pb.cnt["dOut"])])


PHASE_IO = {
    "A": dict(
        ins={"x_fm": ([D, TPC], F32), "aWx": ([D, 96], F32), "aWdt": ([RK, D], F32),
             "aBdt": ([128, DC], F32), "aAlog": ([D, NST], F32), "aD": ([128, DC], F32)},
        outs={"x1": ([D, TPC], F32), "ln1b": ([D, TPC], BF16)},
        internal={},
        emit=emit_phase_A),
    "B": dict(
        ins={"x1": ([D, TPC], F32), "ln1b": ([D, TPC], BF16),
             "bW1T": ([NE, D, 3584], F32), "bW3T": ([NE, D, 3584], F32),
             "bW2T": ([NE, 3584, D], F32), "bGwT": ([D, NE], F32)},
        outs={"x2": ([D, TPC], F32), "ln2b": ([D, TPC], BF16)},
        internal={"bW1C": ([NE, D, 3584], BF16), "bW3C": ([NE, D, 3584], BF16),
                  "bW2C": ([NE, 3584, D], BF16)},
        emit=emit_phase_B),
    "C": dict(
        ins={"x2": ([D, TPC], F32), "ln2b": ([D, TPC], BF16),
             "cBwT": ([D, 256], F32), "cCwT": ([256, D], F32),
             "cLogLam": ([128, 2], F32), "cD1": ([128, DC], F32)},
        outs={"x3": ([D, TPC], F32), "ln3b": ([D, TPC], BF16)},
        internal={},
        emit=emit_phase_C),
    "D": dict(
        ins={"x3": ([D, TPC], F32), "ln3b": ([D, TPC], BF16),
             "dW1T": ([D, FH], F32), "dW2T": ([D, FH], F32),
             "dH1T": ([FH, FH], F32), "dH2T": ([FH, FH], F32),
             "dW3T": ([FH, D], F32),
             "dS1c": ([128, FH // 128], F32), "dS2c": ([128, FH // 128], F32),
             "dHS1c": ([128, FH // 128], F32), "dHS2c": ([128, FH // 128], F32),
             "dS3c": ([128, DC], F32)},
        outs={"out_fm": ([D, TPC], F32)},
        internal={"dW1C": ([D, FH], BF16), "dW2C": ([D, FH], BF16),
                  "dH1C": ([FH, FH], BF16), "dH2C": ([FH, FH], BF16),
                  "dW3C": ([FH, D], BF16)},
        emit=emit_phase_D),
}


def build_phases(phases, fused=False):
    nc = bass.Bass("TRN2", target_bir_lowering=False)
    io = {}
    consts = make_consts()
    for k, v in consts.items():
        io[k] = nc.dram_tensor(k, list(v.shape), BF16 if v.dtype == BF16NP else F32,
                               kind="ExternalInput").ap()
    later_ins = set()
    if fused:
        for ph in phases:
            later_ins |= set(PHASE_IO[ph]["ins"].keys())
    for ph in phases:
        spec = PHASE_IO[ph]
        for name, (shape, dt) in spec["ins"].items():
            if name not in io:
                io[name] = nc.dram_tensor(name, shape, dt, kind="ExternalInput").ap()
        for name, (shape, dt) in spec["outs"].items():
            kind = "Internal" if (fused and name in later_ins) else "ExternalOutput"
            io[name] = nc.dram_tensor(name, shape, dt, kind=kind).ap()
        for name, (shape, dt) in spec.get("internal", {}).items():
            io[name] = nc.dram_tensor(name, shape, dt, kind="Internal").ap()
        if DEBUG and ph == "B":
            for nm, shp, ddt in [("dbgB_w", [128, DC, FQ], BF16), ("dbgB_gfm", [8, HT], BF16),
                                 ("dbgB_sct", [128, NBLK, 8], F32), ("dbgB_xa", [128, DC, HT], F32),
                                 ("dbgB_rt8", [128, 10, 8], F32), ("dbgB_rtc", [128, 12], F32)]:
                io[nm] = nc.dram_tensor(nm, shp, ddt, kind="ExternalOutput").ap()
        if DEBUG and ph == "A":
            for nm, shp, ddt in [("dbg_ln0", [128, DC, CH], BF16), ("dbg_dt", [128, DC, CH], F32),
                                 ("dbg_dtr", [64, CH], BF16), ("dbg_tpa", [128, CH], F32),
                                 ("dbg_ps", [128, CH], F32),
                                 ("dbg_Bb", [128, NST, CH], BF16), ("dbg_Ca", [128, CH, NST], BF16),
                                 ("dbg_h", [128, CH, NST], BF16), ("dbg_y", [128, DC * CH], F32)]:
                io[nm] = nc.dram_tensor(nm, shp, ddt, kind="ExternalOutput").ap()
    pb = PB(nc)
    pb.meta = {}
    if fused and "A" in phases and "B" in phases:
        def _prelude(pb2, io=io):
            if "B" in phases:
                for ex in range(NE):
                    pb2.dma("gpsimd", io["bW1C"][ex], io["bW1T"][ex], "dWC")
                    pb2.dma("gpsimd", io["bW3C"][ex], io["bW3T"][ex], "dWC")
                    pb2.dma("gpsimd", io["bW2C"][ex], io["bW2T"][ex], "dWC")
                pb2.meta["castB"] = pb2.cnt["dWC"]
            if "D" in phases:
                pb2.dma("gpsimd", io["dW1C"], io["dW1T"], "dWC2")
                pb2.dma("gpsimd", io["dW2C"], io["dW2T"], "dWC2")
                pb2.dma("gpsimd", io["dH1C"], io["dH1T"], "dWC2")
                pb2.dma("gpsimd", io["dH2C"], io["dH2T"], "dWC2")
                pb2.dma("gpsimd", io["dW3C"], io["dW3T"], "dWC2")
                pb2.meta["castD"] = pb2.cnt["dWC2"]
        io["_prelude"] = _prelude
    for i, ph in enumerate(phases):
        with contextlib.ExitStack() as st:
            PHASE_IO[ph]["emit"](pb, nc, st, io)
        if i < len(phases) - 1:
            pb.barrier()
    pb.emit()
    pb.close()
    return nc


def stage_core_A(inputs, c):
    x = np.asarray(inputs["L0_attn_x"])
    xc = x[2 * c:2 * c + 2].reshape(TPC, D).T
    m = {
        "x_fm": np.ascontiguousarray(xc),
        "aWx": np.ascontiguousarray(np.asarray(inputs["L0_attn_W_x"])),
        "aWdt": np.ascontiguousarray(np.asarray(inputs["L0_attn_W_dt"])),
        "aBdt": np.ascontiguousarray(np.asarray(inputs["L0_attn_b_dt"]).reshape(DC, 128).T),
        "aAlog": np.ascontiguousarray(np.asarray(inputs["L0_attn_A_log"])),
        "aD": np.ascontiguousarray(np.asarray(inputs["L0_attn_D"]).reshape(DC, 128).T),
    }
    m.update(make_consts())
    return m


def stage_weights_B(inputs):
    w1 = np.asarray(inputs["L0_ffn_w1"])
    w3 = np.asarray(inputs["L0_ffn_w3"])
    w2 = np.asarray(inputs["L0_ffn_w2"])
    gw = np.asarray(inputs["L0_ffn_gate_w"])
    return {
        "bW1T": np.ascontiguousarray(w1.transpose(0, 2, 1)),
        "bW3T": np.ascontiguousarray(w3.transpose(0, 2, 1)),
        "bW2T": np.ascontiguousarray(w2.transpose(0, 2, 1)),
        "bGwT": np.ascontiguousarray(gw.T),
    }


def stage_weights_C(inputs):
    return {
        "cBwT": np.ascontiguousarray(np.asarray(inputs["L1_attn_B_w"]).T),
        "cCwT": np.ascontiguousarray(np.asarray(inputs["L1_attn_C_w"]).T),
        "cLogLam": np.ascontiguousarray(np.asarray(inputs["L1_attn_log_lambda"]).reshape(2, 128).T),
        "cD1": np.ascontiguousarray(np.asarray(inputs["L1_attn_D"]).reshape(DC, 128).T),
    }


def stage_weights_D(inputs):
    col = lambda v, n: np.ascontiguousarray(np.asarray(v).reshape(n, 128).T)
    return {
        "dW1T": np.ascontiguousarray(np.asarray(inputs["L1_ffn_w1_w"]).T),
        "dW2T": np.ascontiguousarray(np.asarray(inputs["L1_ffn_w2_w"]).T),
        "dH1T": np.ascontiguousarray(np.asarray(inputs["L1_ffn_h1_w"]).T),
        "dH2T": np.ascontiguousarray(np.asarray(inputs["L1_ffn_h2_w"]).T),
        "dW3T": np.ascontiguousarray(np.asarray(inputs["L1_ffn_w3_w"]).T),
        "dS1c": col(inputs["L1_ffn_w1_s"], 32),
        "dS2c": col(inputs["L1_ffn_w2_s"], 32),
        "dHS1c": col(inputs["L1_ffn_h1_s"], 32),
        "dHS2c": col(inputs["L1_ffn_h2_s"], 32),
        "dS3c": col(inputs["L1_ffn_w3_s"], 8),
    }


FUSED = True


def kernel(**inputs):
    consts = make_consts()
    if FUSED:
        nc = build_phases("ABCD", fused=True)
        wB = stage_weights_B(inputs)
        wC = stage_weights_C(inputs)
        wD = stage_weights_D(inputs)
        maps = []
        for c in range(NCORE):
            m = stage_core_A(inputs, c)
            m.update(wB)
            m.update(wC)
            m.update(wD)
            maps.append(m)
        res = run_bass_kernel_spmd(nc, maps, core_ids=list(range(NCORE)))
        outs = []
        for c in range(NCORE):
            o = np.asarray(res.results[c]["out_fm"], np.float32)
            outs.append(o.T.reshape(2, SEQ, D))
        return np.concatenate(outs, axis=0)
    wB = stage_weights_B(inputs)
    wC = stage_weights_C(inputs)
    wD = stage_weights_D(inputs)

    ncA = build_phases("A")
    mapsA = [stage_core_A(inputs, c) for c in range(NCORE)]
    resA = run_bass_kernel_spmd(ncA, mapsA, core_ids=list(range(NCORE)))
    x1s = [np.asarray(resA.results[c]["x1"]) for c in range(NCORE)]
    ln1s = [np.asarray(resA.results[c]["ln1b"]) for c in range(NCORE)]

    ncB = build_phases("B")
    mapsB = [dict(x1=x1s[c], ln1b=ln1s[c], **wB, **consts) for c in range(NCORE)]
    resB = run_bass_kernel_spmd(ncB, mapsB, core_ids=list(range(NCORE)))
    x2s = [np.asarray(resB.results[c]["x2"]) for c in range(NCORE)]
    ln2s = [np.asarray(resB.results[c]["ln2b"]) for c in range(NCORE)]

    ncC = build_phases("C")
    mapsC = [dict(x2=x2s[c], ln2b=ln2s[c], **wC, **consts) for c in range(NCORE)]
    resC = run_bass_kernel_spmd(ncC, mapsC, core_ids=list(range(NCORE)))
    x3s = [np.asarray(resC.results[c]["x3"]) for c in range(NCORE)]
    ln3s = [np.asarray(resC.results[c]["ln3b"]) for c in range(NCORE)]

    ncD = build_phases("D")
    mapsD = [dict(x3=x3s[c], ln3b=ln3s[c], **wD, **consts) for c in range(NCORE)]
    resD = run_bass_kernel_spmd(ncD, mapsD, core_ids=list(range(NCORE)))
    outs = []
    for c in range(NCORE):
        o = np.asarray(resD.results[c]["out_fm"], np.float32)
        outs.append(o.T.reshape(2, SEQ, D))
    return np.concatenate(outs, axis=0)
